# version 19
# baseline (speedup 1.0000x reference)
import os
import numpy as np
import ml_dtypes
import concourse.bass as bass
import concourse.mybir as mybir
from concourse.bass_utils import run_bass_kernel_spmd

F32 = mybir.dt.float32
BF16 = mybir.dt.bfloat16
I32 = mybir.dt.int32
AF = mybir.ActivationFunctionType
ALU = mybir.AluOpType
AX = mybir.AxisListType

_LIMIT = int(os.environ.get("MK_PHASE_LIMIT", "7"))
_P3SUB = int(os.environ.get("MK_P3_SUB", "8"))
_P7SUB = int(os.environ.get("MK_P7_SUB", "4"))
NCORE = 8
T = 1024
D = 4096
W = 1024
H = 16
DH = 64
CO = 3072
NB = 8


class Sem:
    def __init__(self, h):
        self.h = h
        self.n = 0

    def inc(self, ins, k):
        self.n += k
        if ins is not None:
            ins.then_inc(self.h, k)
        return self.n


class Builder:
    def __init__(self, nc, hd, sems):
        self.nc = nc
        self.hd = hd
        self.semh = sems

    def run(self, cur, eng):
        self.cur = cur
        self.E = eng
        self.S = {k: Sem(v) for k, v in self.semh.items()}
        self.emit_all()

    def on(self, w):
        return self.cur == w

    def wt(self, who, sem, val):
        if self.cur == who and val > 0:
            self.E.wait_ge(self.S[sem].h, val)

    def wtv(self, who, mark):
        if mark and self.cur == who and mark[1] > 0:
            self.E.wait_ge(self.S[mark[0]].h, mark[1])

    def inc(self, ins, sem, k=1):
        return self.S[sem].inc(ins, k)

    def cnt(self, sem):
        return self.S[sem].n

    def _tag(self, ins, sem):
        self.S[sem].inc(ins, 1)
        return ins

    def MM(self, out, l, r, start, stop, tp=None):
        ins = None
        if self.cur == 't':
            ins = self.E.matmul(out, l, r, start=start, stop=stop,
                                tile_position=tp)
        return self._tag(ins, 'sP')

    def TR(self, out, in_, ident):
        ins = self.E.transpose(out, in_, ident) if self.cur == 't' else None
        return self._tag(ins, 'sP')

    def ACT(self, out, in_, func, bias=0.0, scale=1.0, accum=None):
        ins = None
        if self.cur == 's':
            n = self.S['sA'].n
            if n > 0:
                self.E.wait_ge(self.S['sA'].h, n)
            ins = self.E.activation(out, in_, func, bias=bias, scale=scale,
                                    accum_out=accum)
        return self._tag(ins, 'sA')

    def V(self, name, *a, **k):
        ins = None
        if self.cur == 'v':
            n = self.S['sV'].n
            if n > 0:
                self.E.wait_ge(self.S['sV'].h, n)
            ins = getattr(self.E, name)(*a, **k)
        return self._tag(ins, 'sV')

    def DMA(self, q, out, in_):
        if self.cur == q:
            return self.E.dma_start(out, in_)
        return None

    def GATHER(self, out, in_, idx):
        if self.cur == 'g':
            return self.E.indirect_dma_start(
                out, None, in_, bass.IndirectOffsetOnAxis(ap=idx, axis=0))
        return None

    def SCATTER(self, out, idx, in_):
        if self.cur == 'g':
            return self.E.indirect_dma_start(
                out, bass.IndirectOffsetOnAxis(ap=idx, axis=0), in_, None)
        return None

    def MEMSET(self, ap, val):
        if self.cur == 'g':
            return self.E.memset(ap, val)
        return None

    def PL(self, name, *a, **k):
        ins = None
        if self.cur == 'g':
            n = self.S['sPL'].n
            if n > 0:
                self.E.wait_ge(self.S['sPL'].h, n)
            ins = getattr(self.E, name)(*a, **k)
        return self._tag(ins, 'sPL')

    def _debug_tail(self, out_r, SR):
        i = self.DMA('y', out_r[:, 0, :], SR[:, 0:1024])
        v = self.inc(i, 'sO0', 16)
        self.wt('y', 'sO0', v)

    def emit_all(self):
        hd = self.hd
        PS = hd['PS']

        def bank(b):
            return PS[:, b, :]

        def bank4(b):
            return PS[:, b, :].rearrange("p (j n) -> p j n", j=4)

        def brow(b):
            return PS[0:1, b, :]

        ptbank = [PS[:, 4, :].bitcast(BF16), PS[:, 7, :].bitcast(BF16)]

        AR1, AR2, AR3, SC, SC2, SR = (hd['AR1'], hd['AR2'], hd['AR3'],
                                      hd['SC'], hd['SC2'], hd['SR'])

        def v_bf(ar, o32, shape):
            ap = ar[:, o32:o32 + (int(np.prod(shape)) // 2)].bitcast(BF16)
            if len(shape) == 1:
                return ap
            pat = "p (" + " ".join(f"d{i}" for i in range(len(shape))) + ") -> p " + \
                  " ".join(f"d{i}" for i in range(len(shape)))
            kw = {f"d{i}": s for i, s in enumerate(shape)}
            return ap.rearrange(pat, **kw)

        def v_f32(ar, o32, shape):
            ap = ar[:, o32:o32 + int(np.prod(shape))]
            if len(shape) == 1:
                return ap
            pat = "p (" + " ".join(f"d{i}" for i in range(len(shape))) + ") -> p " + \
                  " ".join(f"d{i}" for i in range(len(shape)))
            kw = {f"d{i}": s for i, s in enumerate(shape)}
            return ap.rearrange(pat, **kw)

        x_bf = v_bf(AR1, 0, (32, 1026))
        qh_bf = v_bf(AR1, 0, (8, 1024))
        khL = v_bf(AR1, 4096, (8, 1024))
        khH = v_bf(AR1, 8192, (8, 1024))
        vh_bf = v_bf(AR1, 12288, (8, 1024))
        s1 = v_f32(AR1, 0, (32, 512))
        yp = v_f32(AR1, 0, (2, 2, 1024))
        ypbf = v_bf(AR1, 4096, (2, 2, 1024))
        hid_bf = v_bf(AR1, 6144, (8, 1024))
        s2h = v_f32(AR1, 10240, (2, 2, 1024))
        s2bf = v_bf(AR2, 0, (2, 2, 1024))
        sq2bf = v_bf(AR2, 4096, (2, 2, 1024))
        sfin = v_f32(AR1, 0, (2, 2048))
        ofin = v_f32(AR1, 4096, (2, 2048))
        m2rep = v_f32(AR3, 0, (2048,))
        A2rep = v_f32(AR3, 2048, (2048,))

        wq_bf = v_bf(AR2, 0, (8, 1024))
        wk_bf = v_bf(AR2, 4096, (8, 1024))
        wv_bf = v_bf(AR2, 8192, (8, 1024))
        pw_bf = v_bf(AR2, 0, (8, 4096))

        qkv_bf = v_bf(AR3, 0, (24, 1024))
        o_bf = v_bf(AR3, 0, (8, 1024))
        ao_bf = v_bf(AR3, 4096, (8, 1024))
        wo_bf = v_bf(AR3, 8192, (8, 1024))
        xstage = v_f32(AR3, 0, (2, 4, 512))
        m_rep = v_f32(AR3, 8192, (1024,))
        A_rep = v_f32(AR3, 9216, (1024,))
        m2bc = v_f32(AR3, 8192, (2, 512))
        A2bc = v_f32(AR3, 9216, (2, 512))

        wstream = v_bf(SC, 0, (2, 8, 512))
        ystage = v_f32(SC, 0, (2, 2, 512))
        s1bf = v_bf(SC, 2048, (2, 512))
        sqbf = v_bf(SC, 2560, (2, 512))
        up_bf = v_bf(SC, 0, (2, 2, 1024))
        dn_bf = v_bf(SC, 2048, (2, 8, 256))

        p_bf = v_bf(SC2, 0, (8, 128))
        pT4 = v_bf(SC2, 512, (2, 8, 128))
        pT2 = v_bf(SC2, 512, (2, 1024))
        nmx = v_f32(SC2, 1536, (8,))
        ssum8 = v_f32(SC2, 1544, (8,))
        rec8 = v_f32(SC2, 1552, (8,))
        psflat = PS[:, :, :].rearrange("p b n -> p (b n)")

        slotA = SR[0:1, 0:512]
        slotB = SR[0:1, 512:1024]
        slotC = SR[0:1, 1024:1536]

        c_ident = hd['c_ident'][:, :]
        c_mask = hd['c_mask'][:, :]
        c_perm = hd['c_perm']
        c_convb = hd['c_convb']
        c_bq = hd['c_bq']
        c_bqs = hd['c_bqs']
        c_bk = hd['c_bk']
        c_bv = hd['c_bv']
        c_bo = hd['c_bo']
        c_bvb = hd['c_bvb'][:, :]
        c_bvwo = hd['c_bvwo'][:, :]
        c_projb = hd['c_projb']
        c_onesb = hd['c_onesb'][:, :]
        c_onesr = hd['c_onesr'][:, :]
        c_eps = hd['c_eps'][:, :]

        d = hd

        i = self.MEMSET(c_onesb, 1.0); self.inc(i, 'sC')
        i = self.MEMSET(c_onesr, 1.0); self.inc(i, 'sC')
        i = self.MEMSET(c_eps, 1e-5); c_done = self.inc(i, 'sC')

        for nm, dst in [('ident', c_ident), ('mask4', c_mask)]:
            i = self.DMA('g', dst, d[nm][:, :]); self.inc(i, 'sK', 16)
        i = self.DMA('g', c_perm[:, :], d['perm'][:, :]); self.inc(i, 'sK', 16)
        i = self.DMA('g', c_convb[:, :], d['conv_b'][:, :]); self.inc(i, 'sK', 16)
        i = self.DMA('g', c_bq[:, :], d['bq'][:, :]); self.inc(i, 'sK', 16)
        i = self.DMA('g', c_bk[:, :], d['bk'][:, :]); self.inc(i, 'sK', 16)
        i = self.DMA('g', c_bv[:, :], d['bv'][:, :]); self.inc(i, 'sK', 16)
        i = self.DMA('g', c_bo[:, :], d['bo'][:, :]); self.inc(i, 'sK', 16)
        i = self.DMA('g', c_projb[:, :], d['proj_b'][:, :]); cstv = self.inc(i, 'sK', 16)
        cst = ('sK', cstv)

        self.wtv('v', cst)
        i = self.V('tensor_scalar_mul', c_bqs[:, :], c_bq[:, :], 0.125)
        self.V('tensor_copy', c_bvb, hd['c_bv'][:, :])
        v_bqs = self.cnt('sV')

        xT = d['xT'].ap().rearrange("(c p) n -> p c n", p=128)
        xld4 = {}

        def em_xpart(part):
            if part >= 1:
                self.wt('g', 'sX', xld4[part - 1][1])
            i = self.DMA('g', x_bf[:, part * 8:(part + 1) * 8, :],
                         xT[:, part * 8:(part + 1) * 8, 0:1026])
            xld4[part] = ('sX', self.inc(i, 'sX', 16))

        em_xpart(0)

        wld = {}
        peS = {}
        evd = {}
        cwr = [d['conv_w'].ap()[tap].rearrange("(c p) n -> p c n", p=128)
               for tap in range(3)]

        def em_convdma(gs):
            g6, s = gs // 12, gs % 12
            tap, k0 = s // 4, (s % 4) * 8
            if gs >= 2:
                self.wt('g', 'sP', peS[gs - 2])
            sw = 'sW' + str(gs % 2)
            i = self.DMA('g', wstream[:, gs % 2, :, :],
                         cwr[tap][:, k0:k0 + 8, g6 * 512:(g6 + 1) * 512])
            wld[gs] = (sw, self.inc(i, sw, 16))
            if gs < 3:
                em_xpart(gs + 1)

        em_convdma(0)
        em_convdma(1)
        for gs in range(72):
            g6, s = gs // 12, gs % 12
            tap, k0 = s // 4, (s % 4) * 8
            self.wtv('t', wld[gs])
            if gs < 4:
                self.wtv('t', xld4[gs % 4])
            if s == 0 and g6 > 0:
                self.wt('t', 'sA', evd[g6 - 1])
            for c in range(8):
                kc = k0 + c
                for n in range(4):
                    for tt in range(2):
                        self.MM(
                            PS[:, n * 2 + tt, :],
                            wstream[:, gs % 2, c, n * 128:(n + 1) * 128],
                            x_bf[:, kc, tt * 512 + tap: tt * 512 + tap + 512],
                            start=(s == 0 and c == 0),
                            stop=(s == 11 and c == 7))
            peS[gs] = self.cnt('sP')
            if gs + 2 < 72:
                em_convdma(gs + 2)
            if s == 11:
                self.wt('s', 'sP', peS[gs])
                if g6 == 0:
                    self.wtv('s', cst)
                for n in range(4):
                    gn = g6 * 4 + n
                    for tt in range(2):
                        self.ACT(qkv_bf[:, gn, tt * 512:(tt + 1) * 512],
                                 PS[:, n * 2 + tt, :], AF.Identity,
                                 bias=c_convb[:, gn:gn + 1])
                evd[g6] = self.cnt('sA')
                if g6 == 4:
                    i = self.DMA('g', wq_bf,
                                 d['wq'].ap().rearrange("(c p) m -> p c m", p=128))
                    wq_ld = ('sQ', self.inc(i, 'sQ', 16))
                if g6 == 5:
                    self.wtv('g', wq_ld)
                    i = self.DMA('g', wk_bf,
                                 d['wk'].ap().rearrange("(c p) m -> p c m", p=128))
                    wk_ld = ('sQ', self.inc(i, 'sQ', 16))

        if _LIMIT < 2:
            return self._debug_tail(out_rD := d['out_fm'].ap().rearrange("(c p) n -> p c n", p=128), SR)
        self.wtv('g', wk_ld)
        i = self.DMA('g', wv_bf, d['wv'].ap().rearrange("(c p) m -> p c m", p=128))
        wv_ld = ('sQ', self.inc(i, 'sQ', 16))

        self.wt('v', 'sP', peS[71])
        self.V('memset', khL[64:128, :, :], 0)
        ins = self.V('memset', khH[0:64, :, :], 0)
        khz_dv = self.cnt('sV')

        mm2 = {}
        ev2 = {}
        seq = []
        for m in range(8):
            for tt in range(2):
                seq.append(('q', m, tt))
        for m in range(8):
            for tt in range(2):
                seq.append(('k', m, tt))
        for b in range(NB):
            for mh in range(2):
                seq.append(('v', b, mh))
        for i_t, (kind, a, bb) in enumerate(seq):
            bk = i_t % 2
            if i_t == 0:
                self.wtv('t', wq_ld)
                self.wt('t', 'sA', evd[5])
            if i_t == 16:
                self.wtv('t', wk_ld)
            if i_t == 32:
                self.wtv('t', wv_ld)
            if i_t >= 2:
                self.wt('t', 'sA', ev2.get(i_t - 2, 0))
            ins = None
            if kind == 'q':
                for k in range(8):
                    ins = self.MM(bank(bk), wq_bf[:, k, a * 128:(a + 1) * 128],
                                  qkv_bf[:, k, bb * 512:(bb + 1) * 512],
                                  start=(k == 0), stop=(k == 7))
            elif kind == 'k':
                for k in range(8):
                    ins = self.MM(bank(bk), wk_bf[:, k, a * 128:(a + 1) * 128],
                                  qkv_bf[:, 8 + k, bb * 512:(bb + 1) * 512],
                                  start=(k == 0), stop=(k == 7))
            else:
                for k in range(8):
                    ins = self.MM(bank(bk), qkv_bf[:, 16 + k, a * 128:(a + 1) * 128],
                                  wv_bf[:, k, bb * 512:(bb + 1) * 512],
                                  start=(k == 0), stop=(k == 7))
            mm2[i_t] = self.cnt('sP')
            self.wt('s', 'sP', mm2[i_t])
            if i_t == 0:
                self.wt('s', 'sV', v_bqs)
            if kind == 'q':
                ins = self.ACT(qh_bf[:, a, bb * 512:(bb + 1) * 512], bank(bk),
                               AF.Identity, bias=c_bqs[:, a:a + 1], scale=0.125)
            elif kind == 'k':
                self.ACT(khL[0:64, a, bb * 512:(bb + 1) * 512],
                         bank(bk)[0:64, :], AF.Identity,
                         bias=c_bk[0:64, a:a + 1])
                ins = self.ACT(khH[64:128, a, bb * 512:(bb + 1) * 512],
                               bank(bk)[64:128, :], AF.Identity,
                               bias=c_bk[64:128, a:a + 1])
            else:
                ins = self.ACT(vh_bf[:, a, bb * 512:(bb + 1) * 512], bank(bk),
                               AF.Identity)
            ev2[i_t] = self.cnt('sA')

        if _LIMIT < 3:
            return self._debug_tail(d['out_fm'].ap().rearrange("(c p) n -> p c n", p=128), SR)
        self.wt('g', 'sP', mm2[47])
        self.wtv('g', wv_ld)
        i = self.DMA('g', wo_bf, d['wo'].ap().rearrange("(c p) m -> p c m", p=128))
        wo_ld = ('sQ', self.inc(i, 'sQ', 16))

        sc_mm = {}
        red3 = {}
        exp_ac = {}
        p_dv = {}
        tp_mm = {}
        ptc_dv = {}
        o_mm = {}
        o_ev = {}
        for it in range(16):
            b, hg = it // 2, it % 2
            pr = it % 2
            base = pr * 2
            scfl = psflat[:, base * 512:(base + 2) * 512]
            sc8 = scfl.rearrange("p (j n) -> p j n", j=8)
            if it == 0:
                self.wt('t', 'sA', ev2[47])
                self.wt('t', 'sV', khz_dv)
            if it >= 2:
                self.wt('t', 'sA', exp_ac.get(it - 2, 0))
                self.wt('t', 'sV', red3.get(it - 2, 0))
            for hl in range(8):
                h = hg * 8 + hl
                khX = khH if (h % 2) else khL
                self.MM(sc8[:, hl, :],
                        qh_bf[:, h // 2, b * 128:(b + 1) * 128],
                        khX[:, h // 2, b * 128:(b + 1) * 128],
                        start=(hl % 4 == 0), stop=False)
            for hl in range(8):
                self.MM(sc8[:, hl, :], c_ident, c_mask[:, 0:128], start=False,
                        stop=(hl % 4 == 3))
            sc_mm[it] = self.cnt('sP')
            self.wt('v', 'sP', sc_mm[it])
            self.V('reduce_max', nmx[:, it % 2:it % 2 + 1], scfl, axis=AX.X,
                   negate=True)
            red3[it] = self.cnt('sV')
            self.wt('s', 'sV', red3[it])
            if it >= 1:
                self.wt('s', 'sP', tp_mm.get(it - 1, 0))
            self.ACT(p_bf.rearrange("p a b -> p (a b)"), scfl, AF.Exp,
                     bias=nmx[:, it % 2:it % 2 + 1])
            exp_ac[it] = self.cnt('sA')
            self.wt('v', 'sA', exp_ac[it])
            self.V('reduce_sum', ssum8[:, :], p_bf, axis=AX.X)
            self.V('reciprocal', rec8[:, :], ssum8[:, :])
            rcv = self.cnt('sV')
            self.wt('v', 'sV', rcv)
            for hl in range(8):
                self.V('tensor_scalar', p_bf[:, hl, :], p_bf[:, hl, :],
                       rec8[:, hl:hl + 1], None, ALU.mult)
            p_dv[it] = self.cnt('sV')
            self.wt('t', 'sV', p_dv[it])
            if it >= 2:
                self.wt('t', 'sV', ptc_dv.get(it - 2, 0))
            for hl in range(8):
                self.TR(ptbank[pr][:, hl * 128:(hl + 1) * 128], p_bf[:, hl, :],
                        c_ident)
            tp_mm[it] = self.cnt('sP')
            self.wt('v', 'sP', tp_mm[it])
            if it >= 2:
                self.wt('v', 'sP', o_mm.get(it - 2, 0))
            self.V('tensor_copy', pT2[:, pr, :], ptbank[pr][:, :])
            ptc_dv[it] = self.cnt('sV')
            self.wt('t', 'sV', ptc_dv[it])
            if it >= 2:
                self.wt('t', 'sA', o_ev.get(it - 2, 0))
            ob = 5 if pr == 0 else 6
            obv = bank4(ob)
            for hl in range(8):
                h = hg * 8 + hl
                off = (h % 2) * 64
                tp = (0, 64) if (h % 2) else None
                self.MM(obv[off:off + 64, hl // 2, :],
                        vh_bf[:, b, h * 64:(h + 1) * 64],
                        pT4[:, pr, hl, :], start=True, stop=True, tp=tp)
            o_mm[it] = self.cnt('sP')
            self.wt('s', 'sP', o_mm[it])
            self.ACT(o_bf[:, hg * 4:(hg + 1) * 4, b * 128:(b + 1) * 128],
                     obv[:, :, :], AF.Identity)
            o_ev[it] = self.cnt('sA')

        wo_mm = {}
        wo_ev = {}
        self.wt('t', 'sA', o_ev[15])
        self.wt('t', 'sV', red3[15])
        self.wtv('t', wo_ld)
        for mm in range(8):
            for kc in range(8):
                self.MM(PS[:, 1, mm:mm + 1],
                        wo_bf[:, kc, mm * 128:(mm + 1) * 128],
                        c_bvb[:, kc:kc + 1], start=(kc == 0), stop=(kc == 7))
        bvwo_mm = self.cnt('sP')
        self.wt('v', 'sP', bvwo_mm)
        self.V('scalar_tensor_tensor', c_bvwo, PS[:, 1, 0:8], 1.0,
               c_bo[:, :], ALU.mult, ALU.add)
        bvwo_dv = self.cnt('sV')
        for i4 in range(16):
            m, tt = i4 // 2, i4 % 2
            bk = i4 % 2
            if i4 == 0:
                self.wt('t', 'sV', bvwo_dv)
            if i4 >= 2:
                self.wt('t', 'sA', wo_ev.get(i4 - 2, 0))
            ins = None
            for c in range(8):
                ins = self.MM(bank(bk), wo_bf[:, c, m * 128:(m + 1) * 128],
                              o_bf[:, c, tt * 512:(tt + 1) * 512],
                              start=(c == 0), stop=(c == 7))
            wo_mm[i4] = self.cnt('sP')
            self.wt('s', 'sP', wo_mm[i4])
            self.ACT(ao_bf[:, m, tt * 512:(tt + 1) * 512], bank(bk),
                     AF.Identity, bias=c_bvwo[:, m:m + 1])
            wo_ev[i4] = self.cnt('sA')

        self.wt('g', 'sP', mm2[47])
        self.wtv('g', wo_ld)
        i = self.DMA('g', pw_bf.rearrange("p c (u n) -> p c u n", u=2),
                     d['proj_w'].ap().rearrange("(c p) (u n) -> p c u n",
                                                p=128, u=2))
        pw_ld = ('sQ', self.inc(i, 'sQ', 16))

        if _LIMIT < 5:
            return self._debug_tail(d['out_fm'].ap().rearrange("(c p) n -> p c n", p=128), SR)
        xTr = d['xT'].ap().rearrange("(c p) n -> p c n", p=128)
        y_r = d['y_sc'].ap().rearrange("(c p) n -> p c n", p=128)
        pj_mm = {}
        stt_dv = {}
        cast_dv = {}
        sq_ac = {}
        st_mm = {}
        ynorm = {}
        ydma = {}
        xld = {}
        bc_dv = {}

        for tt in range(2):
            def em_xload(grp):
                gi = tt * 8 + grp
                if gi == 0:
                    self.wt('s', 'sP', wo_mm[15])
                if gi >= 2:
                    self.wt('s', 'sV', stt_dv.get(4 * (gi - 2) + 3, -1))
                sx = 'sXa' if gi % 2 == 0 else 'sXb'
                i = self.DMA('s', xstage[:, grp % 2, :, :],
                             xTr[:, grp * 4:(grp + 1) * 4,
                                 1 + tt * 512: 1 + (tt + 1) * 512])
                xld[gi] = (sx, self.inc(i, sx, 16))

            em_xload(0)
            em_xload(1)
            for n in range(32):
                fn = tt * 32 + n
                pb = fn % 2
                if fn == 0:
                    self.wtv('t', pw_ld)
                    self.wt('t', 'sV', red3[15])
                    self.wt('t', 'sA', wo_ev[15])
                if fn >= 2:
                    self.wt('t', 'sV', stt_dv.get(fn - 2, 0))
                if tt == 1 and n == 0:
                    self.wtv('v', ydma[14])
                    self.wtv('v', ydma[15])
                for k in range(8):
                    self.MM(bank(pb), pw_bf[:, k, n * 128:(n + 1) * 128],
                            ao_bf[:, k, tt * 512:(tt + 1) * 512],
                            start=(k == 0), stop=(k == 7))
                pj_mm[fn] = self.cnt('sP')
                if n >= 2:
                    pn = n - 2
                    self.wt('t', 'sA', sq_ac[tt * 32 + pn])
                    if fn == 2:
                        self.wt('t', 'sC', c_done)
                    self.MM(brow(2), c_onesb, s1bf[:, pn % 2, :],
                            start=(pn == 0), stop=(pn == 31))
                    self.MM(brow(3), c_onesb, sqbf[:, pn % 2, :],
                            start=(pn == 0), stop=(pn == 31))
                    st_mm[tt * 32 + pn] = self.cnt('sP')
                self.wt('v', 'sP', pj_mm[fn])
                self.wtv('v', xld[tt * 8 + n // 4])
                self.V('scalar_tensor_tensor', s1[:, n, :], bank(pb),
                       c_projb[:, n:n + 1],
                       xstage[:, (n // 4) % 2, n % 4, :], ALU.add, ALU.add)
                stt_dv[fn] = self.cnt('sV')
                if n % 2 == 1:
                    ti = tt * 16 + n // 2
                    self.wt('y', 'sV', stt_dv[fn])
                    sy = 'sY' + str(ti % 2)
                    i = self.DMA('y',
                                 y_r[:, n - 1:n + 1, tt * 512:(tt + 1) * 512],
                                 s1[:, n - 1:n + 1, :])
                    ydma[ti] = (sy, self.inc(i, sy, 16))
                self.wt('s', 'sV', stt_dv[fn])
                if n >= 2:
                    self.wt('s', 'sP', st_mm.get(tt * 32 + n - 2, 0))
                self.ACT(s1bf[:, n % 2, :], s1[:, n, :], AF.Identity)
                cast_dv[fn] = self.cnt('sA')
                self.ACT(sqbf[:, n % 2, :], s1[:, n, :], AF.Square)
                sq_ac[fn] = self.cnt('sA')
                if n % 4 == 3 and n // 4 + 2 < 8:
                    em_xload(n // 4 + 2)
            for pn in (30, 31):
                self.wt('t', 'sA', sq_ac[tt * 32 + pn])
                self.MM(brow(2), c_onesb, s1bf[:, pn % 2, :],
                        start=(pn == 0), stop=(pn == 31))
                self.MM(brow(3), c_onesb, sqbf[:, pn % 2, :],
                        start=(pn == 0), stop=(pn == 31))
                st_mm[tt * 32 + pn] = self.cnt('sP')
            self.wt('v', 'sP', st_mm[tt * 32 + 31])
            self.V('tensor_copy', slotA, brow(2))
            self.V('tensor_copy', slotB, brow(3))
            self.V('tensor_scalar_mul', slotA, slotA, 1.0 / D)
            self.V('tensor_mul', slotC, slotA, slotA)
            self.V('scalar_tensor_tensor', slotB, slotB, 1.0 / D, slotC,
                   ALU.mult, ALU.subtract)
            var_dv = self.cnt('sV')
            self.wt('s', 'sV', var_dv)
            self.wt('s', 'sC', c_done)
            self.ACT(slotC, slotB, AF.Sqrt, bias=c_eps)
            sd_ac = self.cnt('sA')
            self.wt('v', 'sA', sd_ac)
            self.V('reciprocal', slotB, slotC)
            A_dv = self.cnt('sV')
            self.wt('t', 'sV', A_dv)
            if tt == 0:
                self.wt('t', 'sV', ptc_dv[15])
            self.MM(bank(4), c_onesr, slotA, start=True, stop=True)
            self.MM(bank(5), c_onesr, slotB, start=True, stop=True)
            bc_mm = self.cnt('sP')
            self.wt('v', 'sP', bc_mm)
            self.V('tensor_copy', m_rep[:, tt * 512:(tt + 1) * 512], bank(4))
            self.V('tensor_copy', A_rep[:, tt * 512:(tt + 1) * 512], bank(5))
            bc_dv[tt] = self.cnt('sV')

        upld = {}
        dnld = {}
        gld = {}
        sct = {}
        ypbf_dv = {}
        up_mm = {}
        upm_f = {}
        g_ev = {}
        dnm_f = {}
        s2stt_f = {}
        s2_dv = {}
        c2_dv = {}
        st2_mm = {}
        s2_r = d['s2_sc'].ap()

        self.wtv('g', ydma[30])
        self.wtv('g', ydma[31])
        self.wt('g', 'sP', st_mm[63])

        def em_loads(h):
            pr = str(h % 2)
            if h >= 2:
                self.wt('g', 'sV', s2_dv.get(h - 2, 0))
            for j in range(2):
                i = self.GATHER(yp[:, h % 2, j, :], d['y_sc'][:, :],
                                c_perm[:, 2 * h + j:2 * h + j + 1])
                gv = self.inc(i, 'sG' + pr, 16)
            gld[h] = ('sG' + pr, gv)
            if h >= 2:
                self.wt('g', 'sP', up_mm.get(h - 2, 0))
            i = self.DMA('g', up_bf[:, h % 2, :, :],
                         d['up'].ap()[h].rearrange("(c p) m -> p c m", p=128))
            upld[h] = ('sU' + pr, self.inc(i, 'sU' + pr, 16))
            if h >= 2:
                self.wt('g', 'sP', dnm_f.get((h - 2) * 2 + 1, 0))
            i = self.DMA('g', dn_bf[:, h % 2, :, :],
                         d['down'].ap()[h].rearrange("(c p) m -> p c m", p=128))
            dnld[h] = ('sD' + pr, self.inc(i, 'sD' + pr, 16))

        def em_scatter(h):
            pr = str(h % 2)
            self.wt('g', 'sV', s2_dv[h])
            for j in range(2):
                i = self.SCATTER(d['s2_sc'][:, :],
                                 c_perm[:, 2 * h + j:2 * h + j + 1],
                                 s2h[:, h % 2, j, :])
                sv = self.inc(i, 'sT' + pr, 16)
            sct[h] = ('sT' + pr, sv)

        def em_norm(h):
            self.wtv('v', gld[h])
            if h >= 2:
                self.wt('v', 'sP', up_mm.get(h - 2, 0))
            for j in range(2):
                self.V('tensor_sub', yp[:, h % 2, j, :], yp[:, h % 2, j, :],
                       m_rep)
                self.V('tensor_mul', yp[:, h % 2, j, :], yp[:, h % 2, j, :],
                       A_rep)
            self.V('tensor_copy', ypbf[:, h % 2, :, :], yp[:, h % 2, :, :])
            ypbf_dv[h] = self.cnt('sV')

        def em_ups(h, m, statcb):
            if m == 0:
                self.wtv('t', upld[h])
                self.wt('t', 'sV', ypbf_dv[h])
                if h == 0:
                    self.wt('t', 'sV', red3[15])
                if h >= 1:
                    self.wt('t', 'sV', s2stt_f[(h - 1) * 2 + 1])
            pbk = 0 if m % 2 == 0 else 2
            mf = h * 8 + m
            if mf >= 2:
                self.wt('t', 'sA', g_ev.get(mf - 2, 0))
            for tt in range(2):
                for kc in range(2):
                    self.MM(bank(pbk + tt),
                            up_bf[:, h % 2, kc, m * 128:(m + 1) * 128],
                            ypbf[:, h % 2, kc, tt * 512:(tt + 1) * 512],
                            start=(kc == 0), stop=(kc == 1))
            upm_f[mf] = self.cnt('sP')
            self.wt('s', 'sP', upm_f[mf])
            self.ACT(hid_bf[:, m, :],
                     psflat[:, pbk * 512:(pbk + 2) * 512], AF.Gelu)
            g_ev[mf] = self.cnt('sA')
            if m == 7:
                up_mm[h] = upm_f[mf]
            if statcb:
                statcb(m)

        def em_dn(h):
            for dh in range(2):
                for tt in range(2):
                    dbk = dh * 2 + tt
                    if dh == 0 and tt == 0:
                        self.wt('t', 'sA', g_ev[h * 8 + 7])
                        self.wtv('t', dnld[h])
                    for mc in range(8):
                        self.MM(bank(dbk),
                                dn_bf[:, h % 2, mc, dh * 128:(dh + 1) * 128],
                                hid_bf[:, mc, tt * 512:(tt + 1) * 512],
                                start=(mc == 0), stop=(mc == 7))
                dnm_f[h * 2 + dh] = self.cnt('sP')
                self.wt('v', 'sP', dnm_f[h * 2 + dh])
                if dh == 0 and h >= 2:
                    self.wtv('v', sct.get(h - 2))
                self.V('tensor_add', s2h[:, h % 2, dh, :],
                       psflat[:, dh * 1024:(dh + 1) * 1024],
                       yp[:, h % 2, dh, :])
                s2stt_f[h * 2 + dh] = self.cnt('sV')
            s2_dv[h] = s2stt_f[h * 2 + 1]

        def em_statcasts(h):
            if h >= 2:
                self.wt('v', 'sP', st2_mm.get(h - 2, 0))
            sf = s2h[:, h % 2, :, :].rearrange("p a b -> p (a b)")
            self.V('tensor_copy',
                   s2bf[:, h % 2, :, :].rearrange("p a b -> p (a b)"), sf)
            self.V('tensor_mul',
                   sq2bf[:, h % 2, :, :].rearrange("p a b -> p (a b)"), sf, sf)
            c2_dv[h] = self.cnt('sV')

        def mk_statcb(hs):
            def cb(m):
                q = m // 4
                ch = (m // 2) % 2
                tt = m % 2
                if m == 0:
                    self.wt('t', 'sV', c2_dv[hs])
                    if hs == 0:
                        self.wt('t', 'sV', bc_dv[1])
                first = (hs == 0 and ch == 0)
                last = (hs == H - 1 and ch == 1)
                srcb = s2bf if q == 0 else sq2bf
                self.MM(brow(4 + 2 * q + tt), c_onesb,
                        srcb[:, hs % 2, ch, tt * 512:(tt + 1) * 512],
                        start=first, stop=last)
                if m == 7:
                    st2_mm[hs] = self.cnt('sP')
            return cb


        em_loads(0)
        em_loads(1)
        em_norm(0)
        for h in range(H):
            if h >= 1:
                em_scatter(h - 1)
            if h + 1 < H:
                em_norm(h + 1)
            cb = mk_statcb(h - 1) if h >= 1 else None
            for m in range(8):
                em_ups(h, m, cb)
            em_dn(h)
            em_statcasts(h)
            if h + 2 < H:
                em_loads(h + 2)
        cb = mk_statcb(H - 1)
        for m in range(8):
            cb(m)
        em_scatter(H - 1)

        if _LIMIT < 7:
            return self._debug_tail(d['out_fm'].ap().rearrange("(c p) n -> p c n", p=128), SR)
        a2_dv = {}
        bc2_mm = {}
        bc2_dv = {}
        for tt in range(2):
            self.wt('v', 'sP', st2_mm[H - 1])
            self.V('tensor_copy', slotA, brow(4 + tt))
            self.V('tensor_copy', slotB, brow(6 + tt))
            self.V('tensor_scalar_mul', slotA, slotA, 1.0 / D)
            self.V('tensor_mul', slotC, slotA, slotA)
            ins = self.V('scalar_tensor_tensor', slotB, slotB, 1.0 / D,
                         slotC, ALU.mult, ALU.subtract)
            v2 = self.cnt('sV')
            self.wt('s', 'sV', v2)
            ins = self.ACT(slotC, slotB, AF.Sqrt, bias=c_eps)
            sd2 = self.cnt('sA')
            self.wt('v', 'sA', sd2)
            ins = self.V('reciprocal', slotB, slotC)
            a2_dv[tt] = self.cnt('sV')
            self.wt('t', 'sV', a2_dv[tt])
            if tt == 0:
                self.wt('t', 'sA', g_ev[H * 8 - 1])
                self.wt('t', 'sV', s2stt_f[H * 2 - 1])
            self.MM(bank(tt * 2), c_onesr, slotA, start=True, stop=True)
            ins = self.MM(bank(tt * 2 + 1), c_onesr, slotB, start=True,
                          stop=True)
            bc2_mm[tt] = self.cnt('sP')
            self.wt('v', 'sP', bc2_mm[tt])
            self.V('tensor_copy', m2bc[:, tt, :], bank(tt * 2))
            ins = self.V('tensor_copy', A2bc[:, tt, :], bank(tt * 2 + 1))
            bc2_dv[tt] = self.cnt('sV')

        if _P7SUB < 1:
            return self._debug_tail(d['out_fm'].ap().rearrange("(c p) n -> p c n", p=128), SR)
        s2_rr = d['s2_sc'].ap().rearrange("(c p) n -> p c n", p=128)
        out_r = d['out_fm'].ap().rearrange("(c p) n -> p c n", p=128)
        self.V('tensor_copy', m2rep[:, 0:1024], m2bc.rearrange("p a b -> p (a b)"))
        self.V('tensor_copy', m2rep[:, 1024:2048], m2bc.rearrange("p a b -> p (a b)"))
        self.V('tensor_copy', A2rep[:, 0:1024], A2bc.rearrange("p a b -> p (a b)"))
        self.V('tensor_copy', A2rep[:, 1024:2048],
               A2bc.rearrange("p a b -> p (a b)"))
        rep_dv = self.cnt('sV')
        s2ld = {}
        fin_dv = {}
        outd = {}
        for q in range(16):
            if q == 0:
                self.wtv('g', sct[H - 2])
                self.wtv('g', sct[H - 1])
            if q >= 2 and _P7SUB >= 3:
                self.wt('g', 'sV', fin_dv.get(q - 2, 0))
            sl = 'sL' + str(q % 2)
            i = self.DMA('g', sfin[:, q % 2, :].rearrange("p (c n) -> p c n", c=2),
                         s2_rr[:, 2 * q:2 * q + 2, :])
            s2ld[q] = (sl, self.inc(i, sl, 16))
            if _P7SUB >= 3:
                self.wtv('v', s2ld[q])
                if q == 0:
                    self.wt('v', 'sV', rep_dv)
                if q >= 2 and _P7SUB >= 4:
                    self.wtv('v', outd.get(q - 2))
                self.V('tensor_sub', ofin[:, q % 2, :], sfin[:, q % 2, :], m2rep)
                self.V('tensor_mul', ofin[:, q % 2, :], ofin[:, q % 2, :], A2rep)
                fin_dv[q] = self.cnt('sV')
            if _P7SUB >= 4:
                we = 'y' if q % 2 == 0 else 's'
                self.wt(we, 'sV', fin_dv[q])
                so = 'sO' + str(q % 2)
                i = self.DMA(we, out_r[:, 2 * q:2 * q + 2, :],
                             ofin[:, q % 2, :].rearrange("p (c n) -> p c n", c=2))
                outd[q] = (so, self.inc(i, so, 16))
        if _P7SUB >= 4:
            self.wtv('y', outd[14])
            self.wtv('y', outd[15])
        else:
            self.wtv('y', s2ld[14])
            self.wtv('y', s2ld[15])
            return self._debug_tail(out_r, SR)


def build_nc():
    nc = bass.Bass("TRN2", target_bir_lowering=False)
    hd = {}
    hd['xT'] = nc.dram_tensor("xT", [D, 1152], F32, kind="ExternalInput")
    hd['conv_w'] = nc.dram_tensor("conv_w", [3, D, CO], F32, kind="ExternalInput")
    hd['conv_b'] = nc.dram_tensor("conv_b", [128, 24], F32, kind="ExternalInput")
    hd['wq'] = nc.dram_tensor("wq", [W, W], F32, kind="ExternalInput")
    hd['wk'] = nc.dram_tensor("wk", [W, W], F32, kind="ExternalInput")
    hd['wv'] = nc.dram_tensor("wv", [W, W], F32, kind="ExternalInput")
    hd['bq'] = nc.dram_tensor("bq", [128, 8], F32, kind="ExternalInput")
    hd['bk'] = nc.dram_tensor("bk", [128, 8], F32, kind="ExternalInput")
    hd['bv'] = nc.dram_tensor("bv", [128, 8], F32, kind="ExternalInput")
    hd['wo'] = nc.dram_tensor("wo", [W, W], F32, kind="ExternalInput")
    hd['bo'] = nc.dram_tensor("bo", [128, 8], F32, kind="ExternalInput")
    hd['proj_w'] = nc.dram_tensor("proj_w", [W, D], F32, kind="ExternalInput")
    hd['proj_b'] = nc.dram_tensor("proj_b", [128, 32], F32, kind="ExternalInput")
    hd['up'] = nc.dram_tensor("up", [H, 256, 1024], F32, kind="ExternalInput")
    hd['down'] = nc.dram_tensor("down", [H, 1024, 256], F32, kind="ExternalInput")
    hd['perm'] = nc.dram_tensor("perm", [128, 32], I32, kind="ExternalInput")
    hd['mask4'] = nc.dram_tensor("mask4", [128, 1024], BF16, kind="ExternalInput")
    hd['ident'] = nc.dram_tensor("ident", [128, 128], BF16, kind="ExternalInput")
    hd['out_fm'] = nc.dram_tensor("out_fm", [D, T], F32, kind="ExternalOutput")
    hd['y_sc'] = nc.dram_tensor("y_sc", [D, T], F32, kind="Internal")
    hd['s2_sc'] = nc.dram_tensor("s2_sc", [D, T], F32, kind="Internal")

    from contextlib import ExitStack
    es = ExitStack()
    ent = es.enter_context
    hd['c_ident'] = ent(nc.sbuf_tensor("c_ident", [128, 128], BF16))
    hd['c_mask'] = ent(nc.sbuf_tensor("c_mask", [128, 1024], BF16))
    hd['c_perm'] = ent(nc.sbuf_tensor("c_perm", [128, 32], I32))
    hd['c_convb'] = ent(nc.sbuf_tensor("c_convb", [128, 24], F32))
    hd['c_bq'] = ent(nc.sbuf_tensor("c_bq", [128, 8], F32))
    hd['c_bqs'] = ent(nc.sbuf_tensor("c_bqs", [128, 8], F32))
    hd['c_bk'] = ent(nc.sbuf_tensor("c_bk", [128, 8], F32))
    hd['c_bv'] = ent(nc.sbuf_tensor("c_bv", [128, 8], F32))
    hd['c_bvb'] = ent(nc.sbuf_tensor("c_bvb", [128, 8], BF16))
    hd['c_bvwo'] = ent(nc.sbuf_tensor("c_bvwo", [128, 8], F32))
    hd['c_bo'] = ent(nc.sbuf_tensor("c_bo", [128, 8], F32))
    hd['c_projb'] = ent(nc.sbuf_tensor("c_projb", [128, 32], F32))
    hd['c_onesb'] = ent(nc.sbuf_tensor("c_onesb", [128, 1], BF16))
    hd['c_onesr'] = ent(nc.sbuf_tensor("c_onesr", [1, 128], F32))
    hd['c_eps'] = ent(nc.sbuf_tensor("c_eps", [1, 1], F32))
    hd['AR1'] = ent(nc.sbuf_tensor("AR1", [128, 16448], F32))
    hd['AR2'] = ent(nc.sbuf_tensor("AR2", [128, 16384], F32))
    hd['AR3'] = ent(nc.sbuf_tensor("AR3", [128, 12288], F32))
    hd['SC'] = ent(nc.sbuf_tensor("SC", [128, 4096], F32))
    hd['SC2'] = ent(nc.sbuf_tensor("SC2", [128, 1568], F32))
    hd['SR'] = ent(nc.sbuf_tensor("SR", [128, 1536], F32))
    hd['PS'] = ent(nc.psum_tensor("PS", [128, 8, 512], F32))

    sems = {}
    for name in ['sC', 'sK', 'sX', 'sW0', 'sW1', 'sQ', 'sPL',
                 'sG0', 'sG1', 'sU0', 'sU1', 'sD0', 'sD1', 'sT0', 'sT1',
                 'sXa', 'sXb', 'sY0', 'sY1', 'sL0', 'sL1', 'sO0', 'sO1',
                 'sP', 'sV', 'sA']:
        sems[name] = ent(nc.semaphore(name))

    bld = Builder(nc, hd, sems)
    with nc.Block() as block:
        @block.tensor
        def _(t):
            bld.run('t', t)

        @block.vector
        def _(v):
            bld.run('v', v)

        @block.scalar
        def _(s):
            bld.run('s', s)

        @block.gpsimd
        def _(g):
            bld.run('g', g)

        @block.sync
        def _(sy):
            bld.run('y', sy)
    return nc


def kernel(**inputs):
    x = np.ascontiguousarray(
        np.asarray(inputs["L0_attn_inputs"], np.float32).reshape(8192, D))
    gpad = np.zeros((8320, D), np.float32)
    gpad[1:8193] = x
    xTfull = np.ascontiguousarray(gpad.T)

    causal = np.tril(np.ones((128, 128), bool))
    mask = np.where(causal, np.float32(0.0), np.float32(-1e9))
    mask4 = np.ascontiguousarray(
        np.tile(mask, (1, 8)).astype(ml_dtypes.bfloat16))
    ident = np.eye(128, dtype=ml_dtypes.bfloat16)

    common = {
        "conv_w": np.ascontiguousarray(np.asarray(inputs["L0_attn_conv_w"], np.float32)),
        "conv_b": np.ascontiguousarray(np.asarray(inputs["L0_attn_conv_b"], np.float32).reshape(24, 128).T),
        "wq": np.ascontiguousarray(np.asarray(inputs["L0_attn_wq"], np.float32).reshape(W, W)),
        "wk": np.ascontiguousarray(np.asarray(inputs["L0_attn_wk"], np.float32).reshape(W, W)),
        "wv": np.ascontiguousarray(np.asarray(inputs["L0_attn_wv"], np.float32).reshape(W, W)),
        "bq": np.ascontiguousarray(np.asarray(inputs["L0_attn_bq"], np.float32).reshape(8, 128).T),
        "bk": np.ascontiguousarray(np.asarray(inputs["L0_attn_bk"], np.float32).reshape(8, 128).T),
        "bv": np.ascontiguousarray(np.asarray(inputs["L0_attn_bv"], np.float32).reshape(8, 128).T),
        "wo": np.ascontiguousarray(np.asarray(inputs["L0_attn_wo"], np.float32).reshape(W, W)),
        "bo": np.ascontiguousarray(np.asarray(inputs["L0_attn_bo"], np.float32).reshape(8, 128).T),
        "proj_w": np.ascontiguousarray(np.asarray(inputs["L0_attn_proj_w"], np.float32)),
        "proj_b": np.ascontiguousarray(np.asarray(inputs["L0_attn_proj_b"], np.float32).reshape(32, 128).T),
        "up": np.ascontiguousarray(np.asarray(inputs["L0_ffn_up"], np.float32)),
        "down": np.ascontiguousarray(np.asarray(inputs["L0_ffn_down"], np.float32)),
        "perm": np.ascontiguousarray(np.asarray(inputs["L0_ffn_perm"], np.int32).reshape(32, 128).T),
        "mask4": mask4,
        "ident": ident,
    }
    in_maps = []
    for c in range(NCORE):
        m = dict(common)
        m["xT"] = np.ascontiguousarray(xTfull[:, c * T:c * T + 1152])
        in_maps.append(m)

    nc = build_nc()
    res = run_bass_kernel_spmd(nc, in_maps, core_ids=list(range(NCORE)))
    parts = [res.results[c]["out_fm"] for c in range(NCORE)]
    full = np.concatenate(parts, axis=1)
    return np.ascontiguousarray(full.T).reshape(1, 8192, D).astype(np.float32)


# version 20
# speedup vs baseline: 1.0246x; 1.0246x over previous
import os
import numpy as np
import ml_dtypes
import concourse.bass as bass
import concourse.mybir as mybir
from concourse.bass_utils import run_bass_kernel_spmd

F32 = mybir.dt.float32
BF16 = mybir.dt.bfloat16
I32 = mybir.dt.int32
AF = mybir.ActivationFunctionType
ALU = mybir.AluOpType
AX = mybir.AxisListType

_LIMIT = int(os.environ.get("MK_PHASE_LIMIT", "7"))
_P3SUB = int(os.environ.get("MK_P3_SUB", "8"))
_P7SUB = int(os.environ.get("MK_P7_SUB", "4"))
NCORE = 8
T = 1024
D = 4096
W = 1024
H = 16
DH = 64
CO = 3072
NB = 8


class Sem:
    def __init__(self, h):
        self.h = h
        self.n = 0

    def inc(self, ins, k):
        self.n += k
        if ins is not None:
            ins.then_inc(self.h, k)
        return self.n


class Builder:
    def __init__(self, nc, hd, sems):
        self.nc = nc
        self.hd = hd
        self.semh = sems

    def run(self, cur, eng):
        self.cur = cur
        self.E = eng
        self.S = {k: Sem(v) for k, v in self.semh.items()}
        self.emit_all()

    def on(self, w):
        return self.cur == w

    def wt(self, who, sem, val):
        if self.cur == who and val > 0:
            self.E.wait_ge(self.S[sem].h, val)

    def wtv(self, who, mark):
        if mark and self.cur == who and mark[1] > 0:
            self.E.wait_ge(self.S[mark[0]].h, mark[1])

    def inc(self, ins, sem, k=1):
        return self.S[sem].inc(ins, k)

    def cnt(self, sem):
        return self.S[sem].n

    def _tag(self, ins, sem):
        self.S[sem].inc(ins, 1)
        return ins

    def MM(self, out, l, r, start, stop, tp=None):
        ins = None
        if self.cur == 't':
            ins = self.E.matmul(out, l, r, start=start, stop=stop,
                                tile_position=tp)
        return self._tag(ins, 'sP')

    def TR(self, out, in_, ident):
        ins = self.E.transpose(out, in_, ident) if self.cur == 't' else None
        return self._tag(ins, 'sP')

    def ACT(self, out, in_, func, bias=0.0, scale=1.0, accum=None):
        ins = None
        if self.cur == 's':
            n = self.S['sA'].n
            if n > 0:
                self.E.wait_ge(self.S['sA'].h, n)
            ins = self.E.activation(out, in_, func, bias=bias, scale=scale,
                                    accum_out=accum)
        return self._tag(ins, 'sA')

    def V(self, name, *a, **k):
        ins = None
        if self.cur == 'v':
            n = self.S['sV'].n
            if n > 0:
                self.E.wait_ge(self.S['sV'].h, n)
            ins = getattr(self.E, name)(*a, **k)
        return self._tag(ins, 'sV')

    def DMA(self, q, out, in_):
        if self.cur == q:
            return self.E.dma_start(out, in_)
        return None

    def GATHER(self, out, in_, idx):
        if self.cur == 'g':
            return self.E.indirect_dma_start(
                out, None, in_, bass.IndirectOffsetOnAxis(ap=idx, axis=0))
        return None

    def SCATTER(self, out, idx, in_):
        if self.cur == 'g':
            return self.E.indirect_dma_start(
                out, bass.IndirectOffsetOnAxis(ap=idx, axis=0), in_, None)
        return None

    def MEMSET(self, ap, val):
        if self.cur == 'g':
            return self.E.memset(ap, val)
        return None

    def PL(self, name, *a, **k):
        ins = None
        if self.cur == 'g':
            n = self.S['sPL'].n
            if n > 0:
                self.E.wait_ge(self.S['sPL'].h, n)
            ins = getattr(self.E, name)(*a, **k)
        return self._tag(ins, 'sPL')

    def _debug_tail(self, out_r, SR):
        i = self.DMA('y', out_r[:, 0, :], SR[:, 0:1024])
        v = self.inc(i, 'sO0', 16)
        self.wt('y', 'sO0', v)

    def emit_all(self):
        hd = self.hd
        PS = hd['PS']

        def bank(b):
            return PS[:, b, :]

        def bank4(b):
            return PS[:, b, :].rearrange("p (j n) -> p j n", j=4)

        def brow(b):
            return PS[0:1, b, :]

        ptbank = [PS[:, 4, :].bitcast(BF16), PS[:, 7, :].bitcast(BF16)]

        AR1, AR2, AR3, SC, SC2, SR = (hd['AR1'], hd['AR2'], hd['AR3'],
                                      hd['SC'], hd['SC2'], hd['SR'])

        def v_bf(ar, o32, shape):
            ap = ar[:, o32:o32 + (int(np.prod(shape)) // 2)].bitcast(BF16)
            if len(shape) == 1:
                return ap
            pat = "p (" + " ".join(f"d{i}" for i in range(len(shape))) + ") -> p " + \
                  " ".join(f"d{i}" for i in range(len(shape)))
            kw = {f"d{i}": s for i, s in enumerate(shape)}
            return ap.rearrange(pat, **kw)

        def v_f32(ar, o32, shape):
            ap = ar[:, o32:o32 + int(np.prod(shape))]
            if len(shape) == 1:
                return ap
            pat = "p (" + " ".join(f"d{i}" for i in range(len(shape))) + ") -> p " + \
                  " ".join(f"d{i}" for i in range(len(shape)))
            kw = {f"d{i}": s for i, s in enumerate(shape)}
            return ap.rearrange(pat, **kw)

        x_bf = v_bf(AR1, 0, (32, 1026))
        qh_bf = v_bf(AR1, 0, (8, 1024))
        khL = v_bf(AR1, 4096, (8, 1024))
        khH = v_bf(AR1, 8192, (8, 1024))
        vh_bf = v_bf(AR1, 12288, (8, 1024))
        s1 = v_f32(AR1, 0, (32, 512))
        yp = v_f32(AR1, 0, (2, 2, 1024))
        ypbf = v_bf(AR1, 4096, (2, 2, 1024))
        hid_bf = v_bf(AR1, 6144, (8, 1024))
        s2h = v_f32(AR1, 10240, (2, 2, 1024))
        s2bf = v_bf(AR2, 0, (2, 2, 1024))
        sq2bf = v_bf(AR2, 4096, (2, 2, 1024))
        sfin = v_f32(AR1, 0, (4, 2048))
        ofin = v_f32(AR1, 8192, (4, 2048))
        m2rep = v_f32(AR3, 0, (2048,))
        A2rep = v_f32(AR3, 2048, (2048,))

        wq_bf = v_bf(AR2, 0, (8, 1024))
        wk_bf = v_bf(AR2, 4096, (8, 1024))
        wv_bf = v_bf(AR2, 8192, (8, 1024))
        pw_bf = v_bf(AR2, 0, (8, 4096))

        qkv_bf = v_bf(AR3, 0, (24, 1024))
        o_bf = v_bf(AR3, 0, (8, 1024))
        ao_bf = v_bf(AR3, 4096, (8, 1024))
        wo_bf = v_bf(AR3, 8192, (8, 1024))
        xstage = v_f32(AR3, 0, (2, 4, 512))
        m_rep = v_f32(AR3, 8192, (1024,))
        A_rep = v_f32(AR3, 9216, (1024,))
        m2bc = v_f32(AR3, 8192, (2, 512))
        A2bc = v_f32(AR3, 9216, (2, 512))

        wstream = v_bf(SC, 0, (2, 8, 512))
        ystage = v_f32(SC, 0, (2, 2, 512))
        s1bf = v_bf(SC, 2048, (2, 512))
        sqbf = v_bf(SC, 2560, (2, 512))
        up_bf = v_bf(SC, 0, (2, 2, 1024))
        dn_bf = v_bf(SC, 2048, (2, 8, 256))

        p_bf = v_bf(SC2, 0, (8, 128))
        pT4 = v_bf(SC2, 512, (2, 8, 128))
        pT2 = v_bf(SC2, 512, (2, 1024))
        nmx = v_f32(SC2, 1536, (8,))
        ssum8 = v_f32(SC2, 1544, (8,))
        rec8 = v_f32(SC2, 1552, (8,))
        psflat = PS[:, :, :].rearrange("p b n -> p (b n)")

        slotA = SR[0:1, 0:512]
        slotB = SR[0:1, 512:1024]
        slotC = SR[0:1, 1024:1536]

        c_ident = hd['c_ident'][:, :]
        c_mask = hd['c_mask'][:, :]
        c_perm = hd['c_perm']
        c_convb = hd['c_convb']
        c_bq = hd['c_bq']
        c_bqs = hd['c_bqs']
        c_bk = hd['c_bk']
        c_bv = hd['c_bv']
        c_bo = hd['c_bo']
        c_bvb = hd['c_bvb'][:, :]
        c_bvwo = hd['c_bvwo'][:, :]
        c_projb = hd['c_projb']
        c_onesb = hd['c_onesb'][:, :]
        c_onesr = hd['c_onesr'][:, :]
        c_eps = hd['c_eps'][:, :]

        d = hd

        i = self.MEMSET(c_onesb, 1.0); self.inc(i, 'sC')
        i = self.MEMSET(c_onesr, 1.0); self.inc(i, 'sC')
        i = self.MEMSET(c_eps, 1e-5); c_done = self.inc(i, 'sC')

        for nm, dst in [('ident', c_ident), ('mask4', c_mask)]:
            i = self.DMA('g', dst, d[nm][:, :]); self.inc(i, 'sK', 16)
        i = self.DMA('g', c_perm[:, :], d['perm'][:, :]); self.inc(i, 'sK', 16)
        i = self.DMA('g', c_convb[:, :], d['conv_b'][:, :]); self.inc(i, 'sK', 16)
        i = self.DMA('g', c_bq[:, :], d['bq'][:, :]); self.inc(i, 'sK', 16)
        i = self.DMA('g', c_bk[:, :], d['bk'][:, :]); self.inc(i, 'sK', 16)
        i = self.DMA('g', c_bv[:, :], d['bv'][:, :]); self.inc(i, 'sK', 16)
        i = self.DMA('g', c_bo[:, :], d['bo'][:, :]); self.inc(i, 'sK', 16)
        i = self.DMA('g', c_projb[:, :], d['proj_b'][:, :]); cstv = self.inc(i, 'sK', 16)
        cst = ('sK', cstv)

        self.wtv('v', cst)
        i = self.V('tensor_scalar_mul', c_bqs[:, :], c_bq[:, :], 0.125)
        self.V('tensor_copy', c_bvb, hd['c_bv'][:, :])
        v_bqs = self.cnt('sV')

        xT = d['xT'].ap().rearrange("(c p) n -> p c n", p=128)
        xld4 = {}

        def em_xpart(part):
            if part >= 1:
                self.wt('g', 'sX', xld4[part - 1][1])
            i = self.DMA('g', x_bf[:, part * 8:(part + 1) * 8, :],
                         xT[:, part * 8:(part + 1) * 8, 0:1026])
            xld4[part] = ('sX', self.inc(i, 'sX', 16))

        em_xpart(0)

        wld = {}
        peS = {}
        evd = {}
        cwr = [d['conv_w'].ap()[tap].rearrange("(c p) n -> p c n", p=128)
               for tap in range(3)]

        def em_convdma(gs):
            g6, s = gs // 12, gs % 12
            tap, k0 = s // 4, (s % 4) * 8
            if gs >= 2:
                self.wt('g', 'sP', peS[gs - 2])
            sw = 'sW' + str(gs % 2)
            i = self.DMA('g', wstream[:, gs % 2, :, :],
                         cwr[tap][:, k0:k0 + 8, g6 * 512:(g6 + 1) * 512])
            wld[gs] = (sw, self.inc(i, sw, 16))
            if gs < 3:
                em_xpart(gs + 1)

        em_convdma(0)
        em_convdma(1)
        for gs in range(72):
            g6, s = gs // 12, gs % 12
            tap, k0 = s // 4, (s % 4) * 8
            self.wtv('t', wld[gs])
            if gs < 4:
                self.wtv('t', xld4[gs % 4])
            if s == 0 and g6 > 0:
                self.wt('t', 'sA', evd[g6 - 1])
            for c in range(8):
                kc = k0 + c
                for n in range(4):
                    for tt in range(2):
                        self.MM(
                            PS[:, n * 2 + tt, :],
                            wstream[:, gs % 2, c, n * 128:(n + 1) * 128],
                            x_bf[:, kc, tt * 512 + tap: tt * 512 + tap + 512],
                            start=(s == 0 and c == 0),
                            stop=(s == 11 and c == 7))
            peS[gs] = self.cnt('sP')
            if gs + 2 < 72:
                em_convdma(gs + 2)
            if s == 11:
                self.wt('s', 'sP', peS[gs])
                if g6 == 0:
                    self.wtv('s', cst)
                for n in range(4):
                    gn = g6 * 4 + n
                    for tt in range(2):
                        self.ACT(qkv_bf[:, gn, tt * 512:(tt + 1) * 512],
                                 PS[:, n * 2 + tt, :], AF.Identity,
                                 bias=c_convb[:, gn:gn + 1])
                evd[g6] = self.cnt('sA')
                if g6 == 4:
                    i = self.DMA('g', wq_bf,
                                 d['wq'].ap().rearrange("(c p) m -> p c m", p=128))
                    wq_ld = ('sQ', self.inc(i, 'sQ', 16))
                if g6 == 5:
                    self.wtv('g', wq_ld)
                    i = self.DMA('g', wk_bf,
                                 d['wk'].ap().rearrange("(c p) m -> p c m", p=128))
                    wk_ld = ('sQ', self.inc(i, 'sQ', 16))

        if _LIMIT < 2:
            return self._debug_tail(out_rD := d['out_fm'].ap().rearrange("(c p) n -> p c n", p=128), SR)
        self.wtv('g', wk_ld)
        i = self.DMA('g', wv_bf, d['wv'].ap().rearrange("(c p) m -> p c m", p=128))
        wv_ld = ('sQ', self.inc(i, 'sQ', 16))

        self.wt('v', 'sP', peS[71])
        self.V('memset', khL[64:128, :, :], 0)
        ins = self.V('memset', khH[0:64, :, :], 0)
        khz_dv = self.cnt('sV')

        mm2 = {}
        ev2 = {}
        seq = []
        for m in range(8):
            for tt in range(2):
                seq.append(('q', m, tt))
        for m in range(8):
            for tt in range(2):
                seq.append(('k', m, tt))
        for b in range(NB):
            for mh in range(2):
                seq.append(('v', b, mh))
        for i_t, (kind, a, bb) in enumerate(seq):
            bk = i_t % 2
            if i_t == 0:
                self.wtv('t', wq_ld)
                self.wt('t', 'sA', evd[5])
            if i_t == 16:
                self.wtv('t', wk_ld)
            if i_t == 32:
                self.wtv('t', wv_ld)
            if i_t >= 2:
                self.wt('t', 'sA', ev2.get(i_t - 2, 0))
            ins = None
            if kind == 'q':
                for k in range(8):
                    ins = self.MM(bank(bk), wq_bf[:, k, a * 128:(a + 1) * 128],
                                  qkv_bf[:, k, bb * 512:(bb + 1) * 512],
                                  start=(k == 0), stop=(k == 7))
            elif kind == 'k':
                for k in range(8):
                    ins = self.MM(bank(bk), wk_bf[:, k, a * 128:(a + 1) * 128],
                                  qkv_bf[:, 8 + k, bb * 512:(bb + 1) * 512],
                                  start=(k == 0), stop=(k == 7))
            else:
                for k in range(8):
                    ins = self.MM(bank(bk), qkv_bf[:, 16 + k, a * 128:(a + 1) * 128],
                                  wv_bf[:, k, bb * 512:(bb + 1) * 512],
                                  start=(k == 0), stop=(k == 7))
            mm2[i_t] = self.cnt('sP')
            self.wt('s', 'sP', mm2[i_t])
            if i_t == 0:
                self.wt('s', 'sV', v_bqs)
            if kind == 'q':
                ins = self.ACT(qh_bf[:, a, bb * 512:(bb + 1) * 512], bank(bk),
                               AF.Identity, bias=c_bqs[:, a:a + 1], scale=0.125)
            elif kind == 'k':
                self.ACT(khL[0:64, a, bb * 512:(bb + 1) * 512],
                         bank(bk)[0:64, :], AF.Identity,
                         bias=c_bk[0:64, a:a + 1])
                ins = self.ACT(khH[64:128, a, bb * 512:(bb + 1) * 512],
                               bank(bk)[64:128, :], AF.Identity,
                               bias=c_bk[64:128, a:a + 1])
            else:
                ins = self.ACT(vh_bf[:, a, bb * 512:(bb + 1) * 512], bank(bk),
                               AF.Identity)
            ev2[i_t] = self.cnt('sA')

        if _LIMIT < 3:
            return self._debug_tail(d['out_fm'].ap().rearrange("(c p) n -> p c n", p=128), SR)
        self.wt('g', 'sP', mm2[47])
        self.wtv('g', wv_ld)
        i = self.DMA('g', wo_bf, d['wo'].ap().rearrange("(c p) m -> p c m", p=128))
        wo_ld = ('sQ', self.inc(i, 'sQ', 16))

        sc_mm = {}
        red3 = {}
        exp_ac = {}
        p_dv = {}
        tp_mm = {}
        ptc_dv = {}
        o_mm = {}
        o_ev = {}
        for it in range(16):
            b, hg = it // 2, it % 2
            pr = it % 2
            base = pr * 2
            scfl = psflat[:, base * 512:(base + 2) * 512]
            sc8 = scfl.rearrange("p (j n) -> p j n", j=8)
            if it == 0:
                self.wt('t', 'sA', ev2[47])
                self.wt('t', 'sV', khz_dv)
            if it >= 2:
                self.wt('t', 'sA', exp_ac.get(it - 2, 0))
                self.wt('t', 'sV', red3.get(it - 2, 0))
            for hl in range(8):
                h = hg * 8 + hl
                khX = khH if (h % 2) else khL
                self.MM(sc8[:, hl, :],
                        qh_bf[:, h // 2, b * 128:(b + 1) * 128],
                        khX[:, h // 2, b * 128:(b + 1) * 128],
                        start=(hl % 4 == 0), stop=False)
            for hl in range(8):
                self.MM(sc8[:, hl, :], c_ident, c_mask[:, 0:128], start=False,
                        stop=(hl % 4 == 3))
            sc_mm[it] = self.cnt('sP')
            self.wt('v', 'sP', sc_mm[it])
            self.V('reduce_max', nmx[:, it % 2:it % 2 + 1], scfl, axis=AX.X,
                   negate=True)
            red3[it] = self.cnt('sV')
            self.wt('s', 'sV', red3[it])
            if it >= 1:
                self.wt('s', 'sP', tp_mm.get(it - 1, 0))
            self.ACT(p_bf.rearrange("p a b -> p (a b)"), scfl, AF.Exp,
                     bias=nmx[:, it % 2:it % 2 + 1])
            exp_ac[it] = self.cnt('sA')
            self.wt('v', 'sA', exp_ac[it])
            self.V('reduce_sum', ssum8[:, :], p_bf, axis=AX.X)
            self.V('reciprocal', rec8[:, :], ssum8[:, :])
            rcv = self.cnt('sV')
            self.wt('v', 'sV', rcv)
            for hl in range(8):
                self.V('tensor_scalar', p_bf[:, hl, :], p_bf[:, hl, :],
                       rec8[:, hl:hl + 1], None, ALU.mult)
            p_dv[it] = self.cnt('sV')
            self.wt('t', 'sV', p_dv[it])
            if it >= 2:
                self.wt('t', 'sV', ptc_dv.get(it - 2, 0))
            for hl in range(8):
                self.TR(ptbank[pr][:, hl * 128:(hl + 1) * 128], p_bf[:, hl, :],
                        c_ident)
            tp_mm[it] = self.cnt('sP')
            self.wt('v', 'sP', tp_mm[it])
            if it >= 2:
                self.wt('v', 'sP', o_mm.get(it - 2, 0))
            self.V('tensor_copy', pT2[:, pr, :], ptbank[pr][:, :])
            ptc_dv[it] = self.cnt('sV')
            self.wt('t', 'sV', ptc_dv[it])
            if it >= 2:
                self.wt('t', 'sA', o_ev.get(it - 2, 0))
            ob = 5 if pr == 0 else 6
            obv = bank4(ob)
            for hl in range(8):
                h = hg * 8 + hl
                off = (h % 2) * 64
                tp = (0, 64) if (h % 2) else None
                self.MM(obv[off:off + 64, hl // 2, :],
                        vh_bf[:, b, h * 64:(h + 1) * 64],
                        pT4[:, pr, hl, :], start=True, stop=True, tp=tp)
            o_mm[it] = self.cnt('sP')
            self.wt('s', 'sP', o_mm[it])
            self.ACT(o_bf[:, hg * 4:(hg + 1) * 4, b * 128:(b + 1) * 128],
                     obv[:, :, :], AF.Identity)
            o_ev[it] = self.cnt('sA')

        wo_mm = {}
        wo_ev = {}
        self.wt('t', 'sA', o_ev[15])
        self.wt('t', 'sV', red3[15])
        self.wtv('t', wo_ld)
        for mm in range(8):
            for kc in range(8):
                self.MM(PS[:, 1, mm:mm + 1],
                        wo_bf[:, kc, mm * 128:(mm + 1) * 128],
                        c_bvb[:, kc:kc + 1], start=(kc == 0), stop=(kc == 7))
        bvwo_mm = self.cnt('sP')
        self.wt('v', 'sP', bvwo_mm)
        self.V('scalar_tensor_tensor', c_bvwo, PS[:, 1, 0:8], 1.0,
               c_bo[:, :], ALU.mult, ALU.add)
        bvwo_dv = self.cnt('sV')
        for i4 in range(16):
            m, tt = i4 // 2, i4 % 2
            bk = i4 % 2
            if i4 == 0:
                self.wt('t', 'sV', bvwo_dv)
            if i4 >= 2:
                self.wt('t', 'sA', wo_ev.get(i4 - 2, 0))
            ins = None
            for c in range(8):
                ins = self.MM(bank(bk), wo_bf[:, c, m * 128:(m + 1) * 128],
                              o_bf[:, c, tt * 512:(tt + 1) * 512],
                              start=(c == 0), stop=(c == 7))
            wo_mm[i4] = self.cnt('sP')
            self.wt('s', 'sP', wo_mm[i4])
            self.ACT(ao_bf[:, m, tt * 512:(tt + 1) * 512], bank(bk),
                     AF.Identity, bias=c_bvwo[:, m:m + 1])
            wo_ev[i4] = self.cnt('sA')

        self.wt('g', 'sP', mm2[47])
        self.wtv('g', wo_ld)
        i = self.DMA('g', pw_bf.rearrange("p c (u n) -> p c u n", u=2),
                     d['proj_w'].ap().rearrange("(c p) (u n) -> p c u n",
                                                p=128, u=2))
        pw_ld = ('sQ', self.inc(i, 'sQ', 16))

        if _LIMIT < 5:
            return self._debug_tail(d['out_fm'].ap().rearrange("(c p) n -> p c n", p=128), SR)
        xTr = d['xT'].ap().rearrange("(c p) n -> p c n", p=128)
        y_r = d['y_sc'].ap().rearrange("(c p) n -> p c n", p=128)
        pj_mm = {}
        stt_dv = {}
        cast_dv = {}
        sq_ac = {}
        st_mm = {}
        ynorm = {}
        ydma = {}
        xld = {}
        bc_dv = {}

        for tt in range(2):
            def em_xload(grp):
                gi = tt * 8 + grp
                if gi == 0:
                    self.wt('s', 'sP', wo_mm[15])
                if gi >= 2:
                    self.wt('s', 'sV', stt_dv.get(4 * (gi - 2) + 3, -1))
                sx = 'sXa' if gi % 2 == 0 else 'sXb'
                i = self.DMA('s', xstage[:, grp % 2, :, :],
                             xTr[:, grp * 4:(grp + 1) * 4,
                                 1 + tt * 512: 1 + (tt + 1) * 512])
                xld[gi] = (sx, self.inc(i, sx, 16))

            em_xload(0)
            em_xload(1)
            for n in range(32):
                fn = tt * 32 + n
                pb = fn % 2
                if fn == 0:
                    self.wtv('t', pw_ld)
                    self.wt('t', 'sV', red3[15])
                    self.wt('t', 'sA', wo_ev[15])
                if fn >= 2:
                    self.wt('t', 'sV', stt_dv.get(fn - 2, 0))
                if tt == 1 and n == 0:
                    self.wtv('v', ydma[14])
                    self.wtv('v', ydma[15])
                for k in range(8):
                    self.MM(bank(pb), pw_bf[:, k, n * 128:(n + 1) * 128],
                            ao_bf[:, k, tt * 512:(tt + 1) * 512],
                            start=(k == 0), stop=(k == 7))
                pj_mm[fn] = self.cnt('sP')
                if n >= 2:
                    pn = n - 2
                    self.wt('t', 'sA', sq_ac[tt * 32 + pn])
                    if fn == 2:
                        self.wt('t', 'sC', c_done)
                    self.MM(brow(2), c_onesb, s1bf[:, pn % 2, :],
                            start=(pn == 0), stop=(pn == 31))
                    self.MM(brow(3), c_onesb, sqbf[:, pn % 2, :],
                            start=(pn == 0), stop=(pn == 31))
                    st_mm[tt * 32 + pn] = self.cnt('sP')
                self.wt('v', 'sP', pj_mm[fn])
                self.wtv('v', xld[tt * 8 + n // 4])
                self.V('scalar_tensor_tensor', s1[:, n, :], bank(pb),
                       c_projb[:, n:n + 1],
                       xstage[:, (n // 4) % 2, n % 4, :], ALU.add, ALU.add)
                stt_dv[fn] = self.cnt('sV')
                if n % 2 == 1:
                    ti = tt * 16 + n // 2
                    self.wt('y', 'sV', stt_dv[fn])
                    sy = 'sY' + str(ti % 2)
                    i = self.DMA('y',
                                 y_r[:, n - 1:n + 1, tt * 512:(tt + 1) * 512],
                                 s1[:, n - 1:n + 1, :])
                    ydma[ti] = (sy, self.inc(i, sy, 16))
                self.wt('s', 'sV', stt_dv[fn])
                if n >= 2:
                    self.wt('s', 'sP', st_mm.get(tt * 32 + n - 2, 0))
                self.ACT(s1bf[:, n % 2, :], s1[:, n, :], AF.Identity)
                cast_dv[fn] = self.cnt('sA')
                self.ACT(sqbf[:, n % 2, :], s1[:, n, :], AF.Square)
                sq_ac[fn] = self.cnt('sA')
                if n % 4 == 3 and n // 4 + 2 < 8:
                    em_xload(n // 4 + 2)
            for pn in (30, 31):
                self.wt('t', 'sA', sq_ac[tt * 32 + pn])
                self.MM(brow(2), c_onesb, s1bf[:, pn % 2, :],
                        start=(pn == 0), stop=(pn == 31))
                self.MM(brow(3), c_onesb, sqbf[:, pn % 2, :],
                        start=(pn == 0), stop=(pn == 31))
                st_mm[tt * 32 + pn] = self.cnt('sP')
            self.wt('v', 'sP', st_mm[tt * 32 + 31])
            self.V('tensor_copy', slotA, brow(2))
            self.V('tensor_copy', slotB, brow(3))
            self.V('tensor_scalar_mul', slotA, slotA, 1.0 / D)
            self.V('tensor_mul', slotC, slotA, slotA)
            self.V('scalar_tensor_tensor', slotB, slotB, 1.0 / D, slotC,
                   ALU.mult, ALU.subtract)
            var_dv = self.cnt('sV')
            self.wt('s', 'sV', var_dv)
            self.wt('s', 'sC', c_done)
            self.ACT(slotC, slotB, AF.Sqrt, bias=c_eps)
            sd_ac = self.cnt('sA')
            self.wt('v', 'sA', sd_ac)
            self.V('reciprocal', slotB, slotC)
            A_dv = self.cnt('sV')
            self.wt('t', 'sV', A_dv)
            if tt == 0:
                self.wt('t', 'sV', ptc_dv[15])
            self.MM(bank(4), c_onesr, slotA, start=True, stop=True)
            self.MM(bank(5), c_onesr, slotB, start=True, stop=True)
            bc_mm = self.cnt('sP')
            self.wt('v', 'sP', bc_mm)
            self.V('tensor_copy', m_rep[:, tt * 512:(tt + 1) * 512], bank(4))
            self.V('tensor_copy', A_rep[:, tt * 512:(tt + 1) * 512], bank(5))
            bc_dv[tt] = self.cnt('sV')

        upld = {}
        dnld = {}
        gld = {}
        sct = {}
        ypbf_dv = {}
        up_mm = {}
        upm_f = {}
        g_ev = {}
        dnm_f = {}
        s2stt_f = {}
        s2_dv = {}
        c2_dv = {}
        st2_mm = {}
        s2_r = d['s2_sc'].ap()

        self.wtv('g', ydma[30])
        self.wtv('g', ydma[31])
        self.wt('g', 'sP', st_mm[63])

        def em_loads(h):
            pr = str(h % 2)
            if h >= 2:
                self.wt('g', 'sV', s2_dv.get(h - 2, 0))
            for j in range(2):
                i = self.GATHER(yp[:, h % 2, j, :], d['y_sc'][:, :],
                                c_perm[:, 2 * h + j:2 * h + j + 1])
                gv = self.inc(i, 'sG' + pr, 16)
            gld[h] = ('sG' + pr, gv)
            if h >= 2:
                self.wt('g', 'sP', up_mm.get(h - 2, 0))
            i = self.DMA('g', up_bf[:, h % 2, :, :],
                         d['up'].ap()[h].rearrange("(c p) m -> p c m", p=128))
            upld[h] = ('sU' + pr, self.inc(i, 'sU' + pr, 16))
            if h >= 2:
                self.wt('g', 'sP', dnm_f.get((h - 2) * 2 + 1, 0))
            i = self.DMA('g', dn_bf[:, h % 2, :, :],
                         d['down'].ap()[h].rearrange("(c p) m -> p c m", p=128))
            dnld[h] = ('sD' + pr, self.inc(i, 'sD' + pr, 16))

        def em_scatter(h):
            pr = str(h % 2)
            self.wt('g', 'sV', s2_dv[h])
            for j in range(2):
                i = self.SCATTER(d['s2_sc'][:, :],
                                 c_perm[:, 2 * h + j:2 * h + j + 1],
                                 s2h[:, h % 2, j, :])
                sv = self.inc(i, 'sT' + pr, 16)
            sct[h] = ('sT' + pr, sv)

        def em_norm(h):
            self.wtv('v', gld[h])
            if h >= 2:
                self.wt('v', 'sP', up_mm.get(h - 2, 0))
            for j in range(2):
                self.V('tensor_sub', yp[:, h % 2, j, :], yp[:, h % 2, j, :],
                       m_rep)
                self.V('tensor_mul', yp[:, h % 2, j, :], yp[:, h % 2, j, :],
                       A_rep)
            self.V('tensor_copy', ypbf[:, h % 2, :, :], yp[:, h % 2, :, :])
            ypbf_dv[h] = self.cnt('sV')

        def em_ups(h, m, statcb):
            if m == 0:
                self.wtv('t', upld[h])
                self.wt('t', 'sV', ypbf_dv[h])
                if h == 0:
                    self.wt('t', 'sV', red3[15])
                if h >= 1:
                    self.wt('t', 'sV', s2stt_f[(h - 1) * 2 + 1])
            pbk = 0 if m % 2 == 0 else 2
            mf = h * 8 + m
            if mf >= 2:
                self.wt('t', 'sA', g_ev.get(mf - 2, 0))
            for tt in range(2):
                for kc in range(2):
                    self.MM(bank(pbk + tt),
                            up_bf[:, h % 2, kc, m * 128:(m + 1) * 128],
                            ypbf[:, h % 2, kc, tt * 512:(tt + 1) * 512],
                            start=(kc == 0), stop=(kc == 1))
            upm_f[mf] = self.cnt('sP')
            self.wt('s', 'sP', upm_f[mf])
            self.ACT(hid_bf[:, m, :],
                     psflat[:, pbk * 512:(pbk + 2) * 512], AF.Gelu)
            g_ev[mf] = self.cnt('sA')
            if m == 7:
                up_mm[h] = upm_f[mf]
            if statcb:
                statcb(m)

        def em_dn(h):
            for dh in range(2):
                for tt in range(2):
                    dbk = dh * 2 + tt
                    if dh == 0 and tt == 0:
                        self.wt('t', 'sA', g_ev[h * 8 + 7])
                        self.wtv('t', dnld[h])
                    for mc in range(8):
                        self.MM(bank(dbk),
                                dn_bf[:, h % 2, mc, dh * 128:(dh + 1) * 128],
                                hid_bf[:, mc, tt * 512:(tt + 1) * 512],
                                start=(mc == 0), stop=(mc == 7))
                dnm_f[h * 2 + dh] = self.cnt('sP')
                self.wt('v', 'sP', dnm_f[h * 2 + dh])
                if dh == 0 and h >= 2:
                    self.wtv('v', sct.get(h - 2))
                self.V('tensor_add', s2h[:, h % 2, dh, :],
                       psflat[:, dh * 1024:(dh + 1) * 1024],
                       yp[:, h % 2, dh, :])
                s2stt_f[h * 2 + dh] = self.cnt('sV')
            s2_dv[h] = s2stt_f[h * 2 + 1]

        def em_statcasts(h):
            if h >= 2:
                self.wt('v', 'sP', st2_mm.get(h - 2, 0))
            sf = s2h[:, h % 2, :, :].rearrange("p a b -> p (a b)")
            self.V('tensor_copy',
                   s2bf[:, h % 2, :, :].rearrange("p a b -> p (a b)"), sf)
            self.V('tensor_mul',
                   sq2bf[:, h % 2, :, :].rearrange("p a b -> p (a b)"), sf, sf)
            c2_dv[h] = self.cnt('sV')

        def mk_statcb(hs):
            def cb(m):
                q = m // 4
                ch = (m // 2) % 2
                tt = m % 2
                if m == 0:
                    self.wt('t', 'sV', c2_dv[hs])
                    if hs == 0:
                        self.wt('t', 'sV', bc_dv[1])
                first = (hs == 0 and ch == 0)
                last = (hs == H - 1 and ch == 1)
                srcb = s2bf if q == 0 else sq2bf
                self.MM(brow(4 + 2 * q + tt), c_onesb,
                        srcb[:, hs % 2, ch, tt * 512:(tt + 1) * 512],
                        start=first, stop=last)
                if m == 7:
                    st2_mm[hs] = self.cnt('sP')
            return cb


        em_loads(0)
        em_loads(1)
        em_norm(0)
        for h in range(H):
            if h >= 1:
                em_scatter(h - 1)
            if h + 1 < H:
                em_norm(h + 1)
            cb = mk_statcb(h - 1) if h >= 1 else None
            for m in range(8):
                em_ups(h, m, cb)
            em_dn(h)
            em_statcasts(h)
            if h + 2 < H:
                em_loads(h + 2)
        cb = mk_statcb(H - 1)
        for m in range(8):
            cb(m)
        em_scatter(H - 1)

        if _LIMIT < 7:
            return self._debug_tail(d['out_fm'].ap().rearrange("(c p) n -> p c n", p=128), SR)
        a2_dv = {}
        bc2_mm = {}
        bc2_dv = {}
        for tt in range(2):
            self.wt('v', 'sP', st2_mm[H - 1])
            self.V('tensor_copy', slotA, brow(4 + tt))
            self.V('tensor_copy', slotB, brow(6 + tt))
            self.V('tensor_scalar_mul', slotA, slotA, 1.0 / D)
            self.V('tensor_mul', slotC, slotA, slotA)
            ins = self.V('scalar_tensor_tensor', slotB, slotB, 1.0 / D,
                         slotC, ALU.mult, ALU.subtract)
            v2 = self.cnt('sV')
            self.wt('s', 'sV', v2)
            ins = self.ACT(slotC, slotB, AF.Sqrt, bias=c_eps)
            sd2 = self.cnt('sA')
            self.wt('v', 'sA', sd2)
            ins = self.V('reciprocal', slotB, slotC)
            a2_dv[tt] = self.cnt('sV')
            self.wt('t', 'sV', a2_dv[tt])
            if tt == 0:
                self.wt('t', 'sA', g_ev[H * 8 - 1])
                self.wt('t', 'sV', s2stt_f[H * 2 - 1])
            self.MM(bank(tt * 2), c_onesr, slotA, start=True, stop=True)
            ins = self.MM(bank(tt * 2 + 1), c_onesr, slotB, start=True,
                          stop=True)
            bc2_mm[tt] = self.cnt('sP')
            self.wt('v', 'sP', bc2_mm[tt])
            self.V('tensor_copy', m2bc[:, tt, :], bank(tt * 2))
            ins = self.V('tensor_copy', A2bc[:, tt, :], bank(tt * 2 + 1))
            bc2_dv[tt] = self.cnt('sV')

        if _P7SUB < 1:
            return self._debug_tail(d['out_fm'].ap().rearrange("(c p) n -> p c n", p=128), SR)
        s2_rr = d['s2_sc'].ap().rearrange("(c p) n -> p c n", p=128)
        out_r = d['out_fm'].ap().rearrange("(c p) n -> p c n", p=128)
        self.V('tensor_copy', m2rep[:, 0:1024], m2bc.rearrange("p a b -> p (a b)"))
        self.V('tensor_copy', m2rep[:, 1024:2048], m2bc.rearrange("p a b -> p (a b)"))
        self.V('tensor_copy', A2rep[:, 0:1024], A2bc.rearrange("p a b -> p (a b)"))
        self.V('tensor_copy', A2rep[:, 1024:2048],
               A2bc.rearrange("p a b -> p (a b)"))
        rep_dv = self.cnt('sV')
        s2ld = {}
        fin_dv = {}
        outd = {}
        for q in range(16):
            if q == 0:
                self.wtv('s', sct[H - 2])
                self.wtv('s', sct[H - 1])
            if q >= 4 and _P7SUB >= 3:
                self.wt('s', 'sV', fin_dv.get(q - 4, 0))
            sl = 'sL' + str(q % 4)
            i = self.DMA('s', sfin[:, q % 4, :].rearrange("p (c n) -> p c n", c=2),
                         s2_rr[:, 2 * q:2 * q + 2, :])
            s2ld[q] = (sl, self.inc(i, sl, 16))
            if _P7SUB >= 3:
                self.wtv('v', s2ld[q])
                if q == 0:
                    self.wt('v', 'sV', rep_dv)
                if q >= 4 and _P7SUB >= 4:
                    self.wtv('v', outd.get(q - 4))
                self.V('tensor_sub', ofin[:, q % 4, :], sfin[:, q % 4, :], m2rep)
                self.V('tensor_mul', ofin[:, q % 4, :], ofin[:, q % 4, :], A2rep)
                fin_dv[q] = self.cnt('sV')
            if _P7SUB >= 4:
                self.wt('y', 'sV', fin_dv[q])
                so = 'sO' + str(q % 4)
                i = self.DMA('y', out_r[:, 2 * q:2 * q + 2, :],
                             ofin[:, q % 4, :].rearrange("p (c n) -> p c n", c=2))
                outd[q] = (so, self.inc(i, so, 16))
        if _P7SUB >= 4:
            for qq in range(12, 16):
                self.wtv('y', outd[qq])
        else:
            self.wtv('y', s2ld[14])
            self.wtv('y', s2ld[15])
            return self._debug_tail(out_r, SR)


def build_nc():
    nc = bass.Bass("TRN2", target_bir_lowering=False)
    hd = {}
    hd['xT'] = nc.dram_tensor("xT", [D, 1152], F32, kind="ExternalInput")
    hd['conv_w'] = nc.dram_tensor("conv_w", [3, D, CO], F32, kind="ExternalInput")
    hd['conv_b'] = nc.dram_tensor("conv_b", [128, 24], F32, kind="ExternalInput")
    hd['wq'] = nc.dram_tensor("wq", [W, W], F32, kind="ExternalInput")
    hd['wk'] = nc.dram_tensor("wk", [W, W], F32, kind="ExternalInput")
    hd['wv'] = nc.dram_tensor("wv", [W, W], F32, kind="ExternalInput")
    hd['bq'] = nc.dram_tensor("bq", [128, 8], F32, kind="ExternalInput")
    hd['bk'] = nc.dram_tensor("bk", [128, 8], F32, kind="ExternalInput")
    hd['bv'] = nc.dram_tensor("bv", [128, 8], F32, kind="ExternalInput")
    hd['wo'] = nc.dram_tensor("wo", [W, W], F32, kind="ExternalInput")
    hd['bo'] = nc.dram_tensor("bo", [128, 8], F32, kind="ExternalInput")
    hd['proj_w'] = nc.dram_tensor("proj_w", [W, D], F32, kind="ExternalInput")
    hd['proj_b'] = nc.dram_tensor("proj_b", [128, 32], F32, kind="ExternalInput")
    hd['up'] = nc.dram_tensor("up", [H, 256, 1024], F32, kind="ExternalInput")
    hd['down'] = nc.dram_tensor("down", [H, 1024, 256], F32, kind="ExternalInput")
    hd['perm'] = nc.dram_tensor("perm", [128, 32], I32, kind="ExternalInput")
    hd['mask4'] = nc.dram_tensor("mask4", [128, 1024], BF16, kind="ExternalInput")
    hd['ident'] = nc.dram_tensor("ident", [128, 128], BF16, kind="ExternalInput")
    hd['out_fm'] = nc.dram_tensor("out_fm", [D, T], F32, kind="ExternalOutput")
    hd['y_sc'] = nc.dram_tensor("y_sc", [D, T], F32, kind="Internal")
    hd['s2_sc'] = nc.dram_tensor("s2_sc", [D, T], F32, kind="Internal")

    from contextlib import ExitStack
    es = ExitStack()
    ent = es.enter_context
    hd['c_ident'] = ent(nc.sbuf_tensor("c_ident", [128, 128], BF16))
    hd['c_mask'] = ent(nc.sbuf_tensor("c_mask", [128, 1024], BF16))
    hd['c_perm'] = ent(nc.sbuf_tensor("c_perm", [128, 32], I32))
    hd['c_convb'] = ent(nc.sbuf_tensor("c_convb", [128, 24], F32))
    hd['c_bq'] = ent(nc.sbuf_tensor("c_bq", [128, 8], F32))
    hd['c_bqs'] = ent(nc.sbuf_tensor("c_bqs", [128, 8], F32))
    hd['c_bk'] = ent(nc.sbuf_tensor("c_bk", [128, 8], F32))
    hd['c_bv'] = ent(nc.sbuf_tensor("c_bv", [128, 8], F32))
    hd['c_bvb'] = ent(nc.sbuf_tensor("c_bvb", [128, 8], BF16))
    hd['c_bvwo'] = ent(nc.sbuf_tensor("c_bvwo", [128, 8], F32))
    hd['c_bo'] = ent(nc.sbuf_tensor("c_bo", [128, 8], F32))
    hd['c_projb'] = ent(nc.sbuf_tensor("c_projb", [128, 32], F32))
    hd['c_onesb'] = ent(nc.sbuf_tensor("c_onesb", [128, 1], BF16))
    hd['c_onesr'] = ent(nc.sbuf_tensor("c_onesr", [1, 128], F32))
    hd['c_eps'] = ent(nc.sbuf_tensor("c_eps", [1, 1], F32))
    hd['AR1'] = ent(nc.sbuf_tensor("AR1", [128, 16448], F32))
    hd['AR2'] = ent(nc.sbuf_tensor("AR2", [128, 16384], F32))
    hd['AR3'] = ent(nc.sbuf_tensor("AR3", [128, 12288], F32))
    hd['SC'] = ent(nc.sbuf_tensor("SC", [128, 4096], F32))
    hd['SC2'] = ent(nc.sbuf_tensor("SC2", [128, 1568], F32))
    hd['SR'] = ent(nc.sbuf_tensor("SR", [128, 1536], F32))
    hd['PS'] = ent(nc.psum_tensor("PS", [128, 8, 512], F32))

    sems = {}
    for name in ['sC', 'sK', 'sX', 'sW0', 'sW1', 'sQ', 'sPL',
                 'sG0', 'sG1', 'sU0', 'sU1', 'sD0', 'sD1', 'sT0', 'sT1',
                 'sXa', 'sXb', 'sY0', 'sY1', 'sL0', 'sL1', 'sL2', 'sL3',
                 'sO0', 'sO1', 'sO2', 'sO3',
                 'sP', 'sV', 'sA']:
        sems[name] = ent(nc.semaphore(name))

    bld = Builder(nc, hd, sems)
    with nc.Block() as block:
        @block.tensor
        def _(t):
            bld.run('t', t)

        @block.vector
        def _(v):
            bld.run('v', v)

        @block.scalar
        def _(s):
            bld.run('s', s)

        @block.gpsimd
        def _(g):
            bld.run('g', g)

        @block.sync
        def _(sy):
            bld.run('y', sy)
    return nc


def kernel(**inputs):
    x = np.ascontiguousarray(
        np.asarray(inputs["L0_attn_inputs"], np.float32).reshape(8192, D))
    gpad = np.zeros((8320, D), np.float32)
    gpad[1:8193] = x
    xTfull = np.ascontiguousarray(gpad.T)

    causal = np.tril(np.ones((128, 128), bool))
    mask = np.where(causal, np.float32(0.0), np.float32(-1e9))
    mask4 = np.ascontiguousarray(
        np.tile(mask, (1, 8)).astype(ml_dtypes.bfloat16))
    ident = np.eye(128, dtype=ml_dtypes.bfloat16)

    common = {
        "conv_w": np.ascontiguousarray(np.asarray(inputs["L0_attn_conv_w"], np.float32)),
        "conv_b": np.ascontiguousarray(np.asarray(inputs["L0_attn_conv_b"], np.float32).reshape(24, 128).T),
        "wq": np.ascontiguousarray(np.asarray(inputs["L0_attn_wq"], np.float32).reshape(W, W)),
        "wk": np.ascontiguousarray(np.asarray(inputs["L0_attn_wk"], np.float32).reshape(W, W)),
        "wv": np.ascontiguousarray(np.asarray(inputs["L0_attn_wv"], np.float32).reshape(W, W)),
        "bq": np.ascontiguousarray(np.asarray(inputs["L0_attn_bq"], np.float32).reshape(8, 128).T),
        "bk": np.ascontiguousarray(np.asarray(inputs["L0_attn_bk"], np.float32).reshape(8, 128).T),
        "bv": np.ascontiguousarray(np.asarray(inputs["L0_attn_bv"], np.float32).reshape(8, 128).T),
        "wo": np.ascontiguousarray(np.asarray(inputs["L0_attn_wo"], np.float32).reshape(W, W)),
        "bo": np.ascontiguousarray(np.asarray(inputs["L0_attn_bo"], np.float32).reshape(8, 128).T),
        "proj_w": np.ascontiguousarray(np.asarray(inputs["L0_attn_proj_w"], np.float32)),
        "proj_b": np.ascontiguousarray(np.asarray(inputs["L0_attn_proj_b"], np.float32).reshape(32, 128).T),
        "up": np.ascontiguousarray(np.asarray(inputs["L0_ffn_up"], np.float32)),
        "down": np.ascontiguousarray(np.asarray(inputs["L0_ffn_down"], np.float32)),
        "perm": np.ascontiguousarray(np.asarray(inputs["L0_ffn_perm"], np.int32).reshape(32, 128).T),
        "mask4": mask4,
        "ident": ident,
    }
    in_maps = []
    for c in range(NCORE):
        m = dict(common)
        m["xT"] = np.ascontiguousarray(xTfull[:, c * T:c * T + 1152])
        in_maps.append(m)

    nc = build_nc()
    res = run_bass_kernel_spmd(nc, in_maps, core_ids=list(range(NCORE)))
    parts = [res.results[c]["out_fm"] for c in range(NCORE)]
    full = np.concatenate(parts, axis=1)
    return np.ascontiguousarray(full.T).reshape(1, 8192, D).astype(np.float32)
